# version 13
# baseline (speedup 1.0000x reference)
import numpy as np
import ml_dtypes
import concourse.bass as bass
import concourse.mybir as mybir
from concourse.bass_utils import run_bass_kernel_spmd

F32 = mybir.dt.float32
BF16 = mybir.dt.bfloat16
U32 = mybir.dt.uint32
AF = mybir.ActivationFunctionType
ALU = mybir.AluOpType
AX = mybir.AxisListType

CFG = dict(D=1024, H=4096, E=8, TC=2048, R=64, NCORE=8)


class Prog:

    ENGS = ("T", "V", "C", "G", "Y")

    def __init__(self, nc):
        self.nc = nc
        self.ops = {e: [] for e in self.ENGS}
        self.sems = {}
        self.cnt = {}
        self.last_gate = {e: {} for e in self.ENGS}

    def sem(self, name):
        if name not in self.sems:
            self.sems[name] = self.nc.alloc_semaphore(name)
            self.cnt[name] = 0
        return self.sems[name]

    def wait(self, eng, name, val=None):
        h = self.sem(name)
        v = self.cnt[name] if val is None else val
        if v > 0:
            self.ops[eng].append(("w", h, v))

    def emit(self, eng, fn, inc=None, inc_n=1):
        h = None
        if inc is not None:
            h = self.sem(inc)
            self.cnt[inc] += inc_n
        self.ops[eng].append(("o", fn, h, inc_n))

    def gate(self, engs=None):
        for e in (engs or self.ENGS):
            for name, v in self.cnt.items():
                if v > 0 and self.last_gate[e].get(name, 0) < v:
                    self.ops[e].append(("w", self.sems[name], v))
                    self.last_gate[e][name] = v

    def replay(self, eng_key, eng_obj):
        for item in self.ops[eng_key]:
            if item[0] == "w":
                eng_obj.wait_ge(item[1], item[2])
            else:
                ins = item[1](eng_obj)
                if item[2] is not None:
                    ins.then_inc(item[2], item[3])


def build_program(cfg):
    D, H, E, TC, R, NCORE = (cfg[k] for k in ("D", "H", "E", "TC", "R", "NCORE"))
    Dk = D // 128
    NT = TC // 128
    NG = TC // 512
    HQG = 1024 if H >= 1024 else 512
    HQ = H // HQG
    Hc = HQG // 128
    U = NCORE * TC
    NU = U // 512
    DR = D + R
    QB = 2
    NHC = H // 128
    RSC = float(D) ** -0.5

    nc = bass.Bass("TRN2", target_bir_lowering=False, dynamic_dma_scratch_size=16384)

    x_in = nc.dram_tensor("x_in", [TC, D], F32, kind="ExternalInput").ap()
    Wq_t = nc.dram_tensor("attn_Q", [R, D], F32, kind="ExternalInput").ap()
    Wk_t = nc.dram_tensor("attn_K", [R, D], F32, kind="ExternalInput").ap()
    Wvo_t = nc.dram_tensor("attn_VO", [D, D], F32, kind="ExternalInput").ap()
    Wg_t = nc.dram_tensor("ffn_Wg", [D, E], F32, kind="ExternalInput").ap()
    bg_t = nc.dram_tensor("ffn_bg", [E], F32, kind="ExternalInput").ap()
    W1_t = nc.dram_tensor("ffn_W1", [E, D, H], F32, kind="ExternalInput").ap()
    b1_t = nc.dram_tensor("ffn_b1", [E, H], F32, kind="ExternalInput").ap()
    W2_t = nc.dram_tensor("ffn_W2", [E, H, D], F32, kind="ExternalInput").ap()
    b2_t = nc.dram_tensor("ffn_b2", [E, D], F32, kind="ExternalInput").ap()
    x_out = nc.dram_tensor("x_outp", [TC, D], F32, kind="ExternalOutput").ap()

    cc_in = nc.dram_tensor("cc_in", [TC, D], BF16, kind="Internal")
    ccq_out = [nc.dram_tensor(f"ccq{j}", [NCORE * 512, D], BF16, kind="Internal")
               for j in range(4)]
    cck_in = nc.dram_tensor("cck_in", [R, TC], BF16, kind="Internal")
    cck_out = nc.dram_tensor("cck_out", [R * NCORE, TC], BF16, kind="Internal")

    id_bf = nc.inline_tensor(np.eye(128, dtype=ml_dtypes.bfloat16), "id_bf").ap()
    id_f32 = nc.inline_tensor(np.eye(128, dtype=np.float32), "id_f32").ap()
    ones_bf = nc.inline_tensor(np.ones((1, 128), dtype=ml_dtypes.bfloat16), "ones_bf").ap()
    iota_e = nc.inline_tensor(
        np.tile(np.arange(E, dtype=np.float32), (128, 1)), "iota_e").ap()
    ones_f = nc.inline_tensor(np.ones((1, 128), dtype=np.float32), "ones_f").ap()

    P = Prog(nc)

    import contextlib
    est = contextlib.ExitStack()
    sb = lambda name, shape, dt: est.enter_context(nc.sbuf_tensor(name, shape, dt))

    x_sb = sb("x_sb", [128, NT * D], F32)
    lnT = sb("lnT", [128, Dk * TC], BF16)
    idbf_sb = sb("idbf_sb", [128, 128], BF16)
    idf_sb = sb("idf_sb", [128, 128], F32)
    onesbf_sb = sb("onesbf_sb", [1, 128], BF16)
    onesf_sb = sb("onesf_sb", [1, 128], F32)
    bgr_sb = sb("bgr_sb", [1, E], F32)
    bgb_sb = sb("bgb_sb", [128, E], F32)
    iotae_sb = sb("iotae_sb", [128, E], F32)
    wg_sb = sb("wg_sb", [128, Dk * E], BF16)
    b1T = sb("b1T", [128, NHC * E], F32)
    b2_sb = sb("b2_sb", [E, D], BF16)
    lnbf = [sb("lnbf0", [128, D], BF16), sb("lnbf1", [128, D], BF16)]
    NSTAT = 7
    stat = sb("stat", [128, NSTAT * NT], F32)
    _sq = sb("sq_scr0", [128, D], F32)
    sq_scr = [_sq, _sq]
    g_sb = sb("g_sb", [128, NT * E], F32)

    SUMS, SUMSQ, NEGM, EX2, M2, VAR, RSTD = range(7)
    NGMG, SSUM, IDXF, HALFW = 0, 1, 2, 3
    NGMA, LSUM, RECL = 3 + 1, 3 + 2, 3 + 3

    def st(i):
        return stat[:, i * NT:(i + 1) * NT]

    P.emit("Y", lambda e: e.dma_start(
        out=x_sb[:, :].rearrange("p (t d) -> p t d", d=D),
        in_=x_in.rearrange("(t p) d -> p t d", p=128)),
        inc="s_x", inc_n=16)
    for dst, src in ((idbf_sb, id_bf), (idf_sb, id_f32),
                     (onesbf_sb, ones_bf), (onesf_sb, ones_f), (iotae_sb, iota_e)):
        P.emit("Y", lambda e, d=dst, s=src: e.dma_start(out=d[:, :], in_=s),
               inc="s_const", inc_n=16)
    P.emit("Y", lambda e: e.dma_start(out=bgr_sb[:, :], in_=bg_t.rearrange("(o e) -> o e", o=1)),
           inc="s_const", inc_n=16)
    P.emit("G", lambda e: e.dma_start(
        out=wg_sb[:, :].rearrange("p (c e) -> p c e", e=E),
        in_=Wg_t.rearrange("(c p) e -> p c e", p=128)),
        inc="s_wg", inc_n=16)
    P.emit("G", lambda e: e.dma_start(out=b2_sb[:, :], in_=b2_t), inc="s_b2", inc_n=16)

    def emit_ln(ps_bf, ps_f):
        cln0 = P.cnt.get("cln", 0)
        for t in range(NT):
            if t >= 1:
                P.wait("C", "cln", cln0 + t)
            P.emit("C", lambda e, t=t: e.activation(
                sq_scr[t % 2][:, :], x_sb[:, t * D:(t + 1) * D], AF.Square,
                accum_out=st(SUMSQ)[:, t:t + 1]), inc="cln")
            P.emit("V", lambda e, t=t: e.reduce_sum(
                st(SUMS)[:, t:t + 1], x_sb[:, t * D:(t + 1) * D], axis=AX.X),
                inc="vch")
        P.wait("V", "vch")
        P.emit("V", lambda e: e.tensor_scalar(
            st(NEGM), st(SUMS), -1.0 / D, None, ALU.mult), inc="vch")
        P.wait("V", "cln")
        P.emit("V", lambda e: e.tensor_scalar(
            st(EX2), st(SUMSQ), 1.0 / D, None, ALU.mult), inc="vch")
        P.wait("V", "vch")
        P.emit("V", lambda e: e.tensor_mul(st(M2), st(NEGM), st(NEGM)), inc="vch")
        P.wait("V", "vch")
        P.emit("V", lambda e: e.tensor_sub(st(VAR), st(EX2), st(M2)), inc="vch")
        P.wait("V", "vch")
        P.emit("V", lambda e: e.tensor_scalar(
            st(VAR), st(VAR), 1e-5, None, ALU.add), inc="vvar")
        P.wait("C", "vvar")
        P.emit("C", lambda e: e.activation(
            st(RSTD), st(VAR), AF.Sqrt, scale=1.0), inc="csq")
        P.wait("V", "csq")
        P.emit("V", lambda e: e.reciprocal(st(RSTD), st(RSTD)), inc="vrs")
        P.wait("V", "vrs")
        ttr0 = P.cnt["ttr"] if "ttr" in P.cnt else 0
        vcp0 = P.cnt["vcp"] if "vcp" in P.cnt else 0
        for t in range(NT):
            par = t % 2
            if t >= 2:
                P.wait("V", "ttr", ttr0 + Dk * (t - 1))
            P.emit("V", lambda e, t=t, par=par: e.tensor_scalar(
                lnbf[par][:, :], x_sb[:, t * D:(t + 1) * D],
                st(NEGM)[:, t:t + 1], st(RSTD)[:, t:t + 1], ALU.add, ALU.mult),
                inc="vln")
            P.wait("T", "vln")
            for c in range(Dk):
                n = t * Dk + c
                cp = n % 2
                if n >= 2:
                    P.wait("T", "vcp", vcp0 + n - 1)
                P.emit("T", lambda e, t=t, par=par, c=c, cp=cp: e.transpose(
                    ps_bf[cp][:, 0:128], lnbf[par][:, c * 128:(c + 1) * 128],
                    idbf_sb[:, :]), inc="ttr")
                P.wait("V", "ttr")
                P.emit("V", lambda e, t=t, c=c, cp=cp: e.tensor_copy(
                    lnT[:, c * TC + t * 128: c * TC + (t + 1) * 128],
                    ps_bf[cp][:, 0:128]), inc="vcp")

    def emit_gate(ps_f, maskT, maxv8, idx8, mask_scr):
        P.wait("T", "s_wg", 16)
        cg0 = P.cnt.get("vg", 0)
        for t in range(NT):
            par = t % 2
            if t >= 2:
                P.wait("T", "vg", cg0 + t - 1)
            for c in range(Dk):
                P.emit("T", lambda e, t=t, c=c, par=par: e.matmul(
                    ps_f[par][:, 0:E], lnT[:, c * TC + t * 128: c * TC + (t + 1) * 128],
                    wg_sb[:, c * E:(c + 1) * E],
                    start=(c == 0), stop=(c == Dk - 1)),
                    inc=("tg" if c == Dk - 1 else None))
            P.wait("V", "tg")
            P.emit("V", lambda e, t=t, par=par: e.tensor_add(
                g_sb[:, t * E:(t + 1) * E], ps_f[par][:, 0:E], bgb_sb[:, :]),
                inc="vg")
        P.wait("V", "vg")
        for t in range(NT):
            P.emit("V", lambda e, t=t: e.reduce_max(
                st(NGMG)[:, t:t + 1], g_sb[:, t * E:(t + 1) * E], axis=AX.X,
                negate=True), inc="vngm")
        P.wait("C", "vngm")
        cexp0 = P.cnt.get("cexp", 0)
        for t in range(NT):
            if t >= 1:
                P.wait("C", "cexp", cexp0 + t)
            P.emit("C", lambda e, t=t: e.activation(
                maxv8[:, 0:E], g_sb[:, t * E:(t + 1) * E], AF.Exp,
                bias=st(NGMG)[:, t:t + 1], scale=1.0,
                accum_out=st(SSUM)[:, t:t + 1]), inc="cexp")
        P.wait("V", "cexp")
        P.emit("V", lambda e: e.tensor_scalar(
            st(SSUM), st(SSUM), 2.0, None, ALU.mult), inc="vch")
        P.wait("V", "vch")
        P.emit("V", lambda e: e.reciprocal(st(HALFW), st(SSUM)), inc="vch")
        P.wait("V", "vch")
        vmt0 = P.cnt.get("vmt", 0)
        for t in range(NT):
            par8 = (t % 2) * 8
            if t >= 2:
                P.wait("V", "vch")
            P.emit("V", lambda e, t=t, par8=par8: e.max(
                maxv8[:, par8:par8 + 8], g_sb[:, t * E:(t + 1) * E]), inc="vchA")
            P.wait("V", "vchA")
            P.emit("V", lambda e, t=t, par8=par8: e.max_index(
                idx8[:, par8:par8 + 8], maxv8[:, par8:par8 + 8],
                g_sb[:, t * E:(t + 1) * E]), inc="vchB")
            P.wait("V", "vchB")
            P.emit("V", lambda e, t=t, par8=par8: e.tensor_copy(
                st(IDXF)[:, t:t + 1], idx8[:, par8:par8 + 1]), inc="vch")
        P.wait("V", "vch")
        tmt0 = P.cnt.get("tmt", 0)
        for t in range(NT):
            par = t % 2
            if t >= 2:
                P.wait("V", "tmt", tmt0 + t - 1)
            P.emit("V", lambda e, t=t, par=par: e.tensor_scalar(
                mask_scr[:, par * E:(par + 1) * E], iotae_sb[:, :], st(IDXF)[:, t:t + 1],
                st(HALFW)[:, t:t + 1], ALU.is_equal, ALU.mult), inc="vmk")
            P.wait("T", "vmk")
            if t >= 2:
                P.wait("T", "vmt", vmt0 + t - 1)
            P.emit("T", lambda e, t=t, par=par: e.transpose(
                ps_f[par][0:E, 0:128], mask_scr[:, par * E:(par + 1) * E], idf_sb[:, :]),
                inc="tmt")
            P.wait("V", "tmt")
            P.emit("V", lambda e, t=t, par=par: e.tensor_copy(
                maskT[0:E, t * 128:(t + 1) * 128], ps_f[par][0:E, 0:128]),
                inc="vmt")

    def emit_ffn_main(ps_h, ps_o, ps_x, w1b, w2b, hs, mh, mbc, mrow, maskT,
                      add_bias):
        mm1_0 = P.cnt.get("mm1", 0)
        sil0 = P.cnt.get("sil", 0)
        msk0 = P.cnt.get("msk", 0)
        mm2_0 = P.cnt.get("mm2", 0)
        ev0 = P.cnt.get("evac", 0)
        tbc0 = P.cnt.get("tbc", 0)
        cbc0 = P.cnt.get("cbc", 0)
        n_mm1 = 0
        n_o = 0

        def build_mb(bi):
            bpar = bi % 2
            if bi >= 1:
                P.wait("G", "tbc", tbc0 + bi * NG)
            P.wait("G", "vmt")
            P.emit("G", lambda e, bi=bi: e.dma_start(
                out=mrow[0:1, :], in_=maskT[bi:bi + 1, :]),
                inc=f"gmr{bi % 2}", inc_n=16)
            for gch in range(NG):
                n = bi * NG + gch
                xp = n % 2
                P.wait("T", f"gmr{bpar}")
                if n >= 2:
                    P.wait("T", "cbc", cbc0 + n - 1)
                P.emit("T", lambda e, gch=gch, xp=xp: e.matmul(
                    ps_x[xp][:, :], onesbf_sb[0:1, :],
                    mrow[0:1, gch * 512: (gch + 1) * 512],
                    start=True, stop=True), inc="tbc")
                P.wait("C", "tbc")
                P.wait("C", "msk")
                P.emit("C", lambda e, bpar=bpar, gch=gch, xp=xp: e.activation(
                    mbc[bpar][:, gch * 512:(gch + 1) * 512],
                    ps_x[xp][:, :], AF.Copy), inc="cbc")

        for pre in range(2):
            pei, phq = pre // HQ, pre % HQ
            P.emit("G", lambda e, pei=pei, phq=phq, pre=pre: e.dma_start(
                out=w1b[pre][:, :].rearrange("p (c h) -> p c h", h=HQG),
                in_=W1_t[pei, :, phq * HQG:(phq + 1) * HQG]
                .rearrange("(c p) h -> p c h", p=128)), inc=f"wld{pre}", inc_n=16)
            P.emit("G", lambda e, pei=pei, phq=phq, pre=pre: e.dma_start(
                out=w2b[pre][:, :].rearrange("p (s d) -> p s d", s=Hc),
                in_=W2_t[pei, phq * HQG:(phq + 1) * HQG, :]
                .rearrange("(s p) d -> p s d", p=128)), inc=f"wld{pre}", inc_n=16)
        build_mb(0)
        sched = []
        for it, (ei, hq) in enumerate([(a, b) for a in range(E) for b in range(HQ)]):
            for g in range(NG):
                sched.append((it, ei, hq, g))
        state = {"n_mm1": 0, "n_o": 0}

        def emit_mm2(gi):
            it, ei, hq, g = sched[gi]
            par = it % 2
            mhp = gi % 2
            seed = add_bias and it == 0
            P.wait("T", "msk", msk0 + 4 * (gi + 1))
            for ts in range(4):
                n_o = state["n_o"]
                op = n_o % 2
                tglob = g * 4 + ts
                if n_o >= 2:
                    P.wait("T", "evac", ev0 + n_o - 1)
                for dh in range(2):
                    if seed:
                        P.emit("T", lambda e, op=op, tglob=tglob, dh=dh: e.matmul(
                            ps_o[op][:, dh * 512:(dh + 1) * 512],
                            maskT[0:E, tglob * 128:(tglob + 1) * 128],
                            b2_sb[0:E, dh * 512:(dh + 1) * 512],
                            start=True, stop=False))
                    for h1 in range(Hc):
                        fin = (h1 == Hc - 1 and dh == 1)
                        P.emit("T", lambda e, op=op, mhp=mhp, h1=h1, ts=ts, dh=dh, par=par,
                               seed=seed: e.matmul(
                            ps_o[op][:, dh * 512:(dh + 1) * 512],
                            mh[mhp][:, h1 * 512 + ts * 128: h1 * 512 + ts * 128 + 128],
                            w2b[par][:, h1 * 1024 + dh * 512: h1 * 1024 + (dh + 1) * 512],
                            start=(h1 == 0 and not seed), stop=(h1 == Hc - 1)),
                            inc=("mm2" if fin else None))
                P.wait("V", "mm2")
                if n_o >= NG * 4:
                    P.wait("V", "evac", ev0 + n_o - NG * 4 + 1)
                P.emit("V", lambda e, op=op, tglob=tglob: e.tensor_add(
                    x_sb[:, tglob * D:(tglob + 1) * D],
                    x_sb[:, tglob * D:(tglob + 1) * D], ps_o[op][:, :]),
                    inc="evac")
                state["n_o"] += 1

        for gi, (it, ei, hq, g) in enumerate(sched):
            par = it % 2
            if g == 0:
                if hq == 0 and ei + 1 < E:
                    build_mb(ei + 1)
                if it >= 2:
                    P.wait("G", "mm2", mm2_0 + (it - 1) * NG * 4)
                    P.emit("G", lambda e, ei=ei, hq=hq, par=par: e.dma_start(
                        out=w1b[par][:, :].rearrange("p (c h) -> p c h", h=HQG),
                        in_=W1_t[ei, :, hq * HQG:(hq + 1) * HQG]
                        .rearrange("(c p) h -> p c h", p=128)), inc=f"wld{par}", inc_n=16)
                    P.emit("G", lambda e, ei=ei, hq=hq, par=par: e.dma_start(
                        out=w2b[par][:, :].rearrange("p (s d) -> p s d", s=Hc),
                        in_=W2_t[ei, hq * HQG:(hq + 1) * HQG, :]
                        .rearrange("(s p) d -> p s d", p=128)), inc=f"wld{par}", inc_n=16)
                P.wait("T", f"wld{par}")
                if hq == 0:
                    P.wait("V", "cbc", cbc0 + (ei + 1) * NG)
            mhp = gi % 2
            for h1 in range(Hc):
                n_mm1 = state["n_mm1"]
                hp = n_mm1 % 2
                if n_mm1 >= 2:
                    P.wait("T", "sil", sil0 + n_mm1 - 1)
                for c in range(Dk):
                    P.emit("T", lambda e, par=par, c=c, h1=h1, g=g, hp=hp: e.matmul(
                        ps_h[hp][:, :],
                        w1b[par][:, c * HQG + h1 * 128: c * HQG + (h1 + 1) * 128],
                        lnT[:, c * TC + g * 512: c * TC + (g + 1) * 512],
                        start=(c == 0), stop=(c == Dk - 1)),
                        inc=("mm1" if c == Dk - 1 else None))
                P.wait("C", "mm1")
                if n_mm1 >= 2:
                    P.wait("C", "msk", msk0 + n_mm1 - 1)
                hsp = n_mm1 % 2
                bidx = (hq * Hc + h1) * E + ei
                P.emit("C", lambda e, hp=hp, hsp=hsp, bidx=bidx: e.activation(
                    hs[hsp][:, :], ps_h[hp][:, :], AF.Silu,
                    bias=b1T[:, bidx:bidx + 1], scale=1.0), inc="sil")
                P.wait("V", "sil")
                if gi >= 2:
                    P.wait("V", "mm2", mm2_0 + 4 * (gi - 1))
                P.emit("V", lambda e, hsp=hsp, mhp=mhp, h1=h1, ei=ei, g=g: e.tensor_tensor(
                    mh[mhp][:, h1 * 512:(h1 + 1) * 512], hs[hsp][:, :],
                    mbc[ei % 2][:, g * 512:(g + 1) * 512],
                    ALU.mult), inc="msk")
                state["n_mm1"] += 1
            if gi >= 1:
                emit_mm2(gi - 1)
        emit_mm2(len(sched) - 1)

    _phase_ctr = [0]

    def emit_ffn_phase():
        import contextlib as _ctx
        _phase_ctr[0] += 1
        sfx = f"_f{_phase_ctr[0]}"
        with _ctx.ExitStack() as fes:
            pst = lambda n, sh, dt: fes.enter_context(nc.psum_tensor(n + sfx, sh, dt))
            sbt = lambda n, sh, dt: fes.enter_context(nc.sbuf_tensor(n + sfx, sh, dt))
            ps_h0 = pst("ps_h0", [128, 512], F32)
            ps_h1 = pst("ps_h1", [128, 512], F32)
            ps_o0 = pst("ps_o0", [128, 1024], F32)
            ps_o1 = pst("ps_o1", [128, 1024], F32)
            w1b0 = sbt("w1b0", [128, Dk * HQG], BF16)
            w1b1 = sbt("w1b1", [128, Dk * HQG], BF16)
            w2b0 = sbt("w2b0", [128, Hc * D], BF16)
            w2b1 = sbt("w2b1", [128, Hc * D], BF16)
            hs0 = sbt("hs0", [128, 512], BF16)
            hs1 = sbt("hs1", [128, 512], BF16)
            mh0 = sbt("mh0", [128, Hc * 512], BF16)
            mh1 = sbt("mh1", [128, Hc * 512], BF16)
            mbc0 = sbt("mbc0", [128, TC], BF16)
            mbc1 = sbt("mbc1", [128, TC], BF16)
            maskT = sbt("maskT", [E, TC], BF16)
            mrow = sbt("mrow", [1, TC], BF16)
            maxv8 = sbt("maxv8", [128, 8 * 2], F32)
            idx8 = sbt("idx8", [128, 8 * 2], mybir.dt.uint16)
            mask_scr = sbt("mask_scr", [128, E * 2], F32)
            ps_h = [ps_h0, ps_h1]
            ps_o = [ps_o0, ps_o1]
            P.gate()
            with _ctx.ExitStack() as lns:
                ps_bf = [lns.enter_context(nc.psum_tensor("ps_bfa" + sfx, [128, 512], BF16)),
                         lns.enter_context(nc.psum_tensor("ps_bfb" + sfx, [128, 512], BF16))]
                emit_ln(ps_bf, ps_h)
            P.gate(("T", "C", "V"))
            emit_gate(ps_h, maskT, maxv8, idx8, mask_scr)
            P.gate(("T", "C", "V", "G"))
            ps_x = [fes.enter_context(nc.psum_tensor("ps_x0" + sfx, [128, 512], F32)),
                    fes.enter_context(nc.psum_tensor("ps_x1" + sfx, [128, 512], F32))]
            emit_ffn_main(ps_h, ps_o, ps_x, [w1b0, w1b1], [w2b0, w2b1],
                          [hs0, hs1], [mh0, mh1], [mbc0, mbc1], mrow, maskT,
                          add_bias=True)

    def emit_attn_phase():
        import contextlib as _ctx
        with _ctx.ExitStack() as aes:
            pst = lambda n, sh, dt: aes.enter_context(nc.psum_tensor(n, sh, dt))
            sbt = lambda n, sh, dt: aes.enter_context(nc.sbuf_tensor(n, sh, dt))
            ps_sc0 = pst("ps_sc0", [128, 512], F32)
            ps_sc1 = pst("ps_sc1", [128, 512], F32)
            ps_at0 = pst("ps_at0", [128, 1024], F32)
            ps_at1 = pst("ps_at1", [128, 1024], F32)
            ps_pt0 = pst("ps_pt0", [128, 512], BF16)
            ps_pt1 = pst("ps_pt1", [128, 512], BF16)
            qT_sb = sbt("qT_sb", [R, TC], BF16)
            lbuf = sbt("lbuf", [128, NT * NU], F32)
            mbufA = sbt("mbufA", [128, NT * NU], F32)
            kT_all = sbt("kT_all", [R, U], BF16)
            vstg = [sbt("vstg0", [128, D], BF16), sbt("vstg1", [128, D], BF16)]
            kTc_sb = sbt("kTc_sb", [R, TC], BF16)
            ps_sc = [ps_sc0, ps_sc1]
            ps_at = [ps_at0, ps_at1]
            ps_pt = [ps_pt0, ps_pt1]
            pes = aes.enter_context(_ctx.ExitStack())
            qwb = pes.enter_context(nc.sbuf_tensor("qwb", [R, D], BF16))
            kwb = pes.enter_context(nc.sbuf_tensor("kwb", [R, D], BF16))
            vob = pes.enter_context(nc.sbuf_tensor("vob", [128, Dk * D], BF16))
            QTb = pes.enter_context(nc.sbuf_tensor("QTb", [128, Dk * R], BF16))
            KTb = pes.enter_context(nc.sbuf_tensor("KTb", [128, Dk * R], BF16))
            P.emit("G", lambda e: e.dma_start(out=qwb[:, :], in_=Wq_t), inc="aql", inc_n=16)
            P.emit("G", lambda e: e.dma_start(out=kwb[:, :], in_=Wk_t), inc="aql", inc_n=16)
            P.emit("G", lambda e: e.dma_start(
                out=vob[:, :].rearrange("p (c d) -> p c d", d=D),
                in_=Wvo_t.rearrange("(c p) d -> p c d", p=128)),
                inc="aql", inc_n=16)

            P.gate()
            emit_ln(ps_pt, ps_sc)
            P.gate(("T", "C", "V"))

            P.wait("T", "aql")
            for i, (wsrc, wdst) in enumerate(((qwb, QTb), (kwb, KTb))):
                for c in range(Dk):
                    n = i * Dk + c
                    pp = n % 2
                    if n >= 2:
                        P.wait("T", "vqt", n - 1)
                    P.emit("T", lambda e, wsrc=wsrc, c=c, pp=pp: e.transpose(
                        ps_pt[pp][:, 0:R], wsrc[0:R, c * 128:(c + 1) * 128],
                        idbf_sb[0:R, 0:R]), inc="tqt")
                    P.wait("V", "tqt")
                    P.emit("V", lambda e, wdst=wdst, c=c, pp=pp: e.tensor_copy(
                        wdst[:, c * R:(c + 1) * R], ps_pt[pp][:, 0:R]), inc="vqt")
            P.gate(("T", "V"))

            for tch in range(TC // 512):
                pp = tch % 2
                if tch >= 2:
                    P.wait("T", "cq", tch - 1)
                for c in range(Dk):
                    P.emit("T", lambda e, tch=tch, c=c, pp=pp: e.matmul(
                        ps_sc[pp][0:R, 0:512], QTb[:, c * R:(c + 1) * R],
                        lnT[:, c * TC + tch * 512: c * TC + (tch + 1) * 512],
                        start=(c == 0), stop=(c == Dk - 1)),
                        inc=("tq" if c == Dk - 1 else None))
                P.wait("C", "tq")
                P.emit("C", lambda e, tch=tch, pp=pp: e.activation(
                    qT_sb[0:R, tch * 512:(tch + 1) * 512], ps_sc[pp][0:R, 0:512],
                    AF.Copy, scale=RSC), inc="cq")
            P.gate(("T", "C"))

            for tch in range(TC // 512):
                pp = tch % 2
                if tch >= 2:
                    P.wait("T", "vkp", tch - 1)
                for c in range(Dk):
                    P.emit("T", lambda e, tch=tch, c=c, pp=pp: e.matmul(
                        ps_sc[pp][0:R, 0:512], KTb[:, c * R:(c + 1) * R],
                        lnT[:, c * TC + tch * 512: c * TC + (tch + 1) * 512],
                        start=(c == 0), stop=(c == Dk - 1)),
                        inc=("tkp" if c == Dk - 1 else None))
                P.wait("V", "tkp")
                P.emit("V", lambda e, tch=tch, pp=pp: e.tensor_copy(
                    kTc_sb[0:R, tch * 512:(tch + 1) * 512], ps_sc[pp][0:R, 0:512]),
                    inc="vkp")
            P.wait("Y", "vkp")
            P.emit("Y", lambda e: e.dma_start(
                out=cck_in.ap(), in_=kTc_sb[:, :]), inc="yccin", inc_n=16)
            P.wait("G", "yccin")
            P.emit("G", lambda e: e.collective_compute(
                "AllGather", ALU.bypass,
                replica_groups=[list(range(NCORE))],
                ins=[cck_in.ap().opt()], outs=[cck_out.ap().opt()]), inc="gcc", inc_n=1)
            cin_rows = cc_in.ap().rearrange("(t p) d -> t p d", p=128)
            for t in range(NT):
                pp = t % 2
                if t >= 2:
                    P.wait("T", "cvp", t - 1)
                for dh in range(2):
                    for c in range(Dk):
                        P.emit("T", lambda e, t=t, c=c, dh=dh, pp=pp: e.matmul(
                            ps_at[pp][:, dh * 512:(dh + 1) * 512],
                            lnT[:, c * TC + t * 128: c * TC + (t + 1) * 128],
                            vob[:, c * D + dh * 512: c * D + (dh + 1) * 512],
                            start=(c == 0), stop=(c == Dk - 1)),
                            inc=("tvp" if (c == Dk - 1 and dh == 1) else None))
                P.wait("C", "tvp")
                if t >= 2:
                    P.wait("C", f"yst{t % 2}")
                P.emit("C", lambda e, t=t, pp=pp: e.activation(
                    vstg[t % 2][:, :], ps_at[pp][:, :], AF.Copy),
                    inc="cvp")
                P.wait("Y", "cvp")
                P.emit("Y", lambda e, t=t: e.dma_start(
                    out=cin_rows[t], in_=vstg[t % 2][:, :]),
                    inc=f"yst{t % 2}", inc_n=16)
                if t % 4 == 3:
                    j = t // 4
                    P.wait("G", "yst0", 16 * (t // 2 + 1))
                    P.wait("G", "yst1", 16 * (t // 2 + 1))
                    P.emit("G", lambda e, j=j: e.collective_compute(
                        "AllGather", ALU.bypass,
                        replica_groups=[list(range(NCORE))],
                        ins=[cc_in.ap()[j * 512:(j + 1) * 512, :].opt()],
                        outs=[ccq_out[j].ap().opt()]), inc="gccq", inc_n=1)
            P.wait("Y", "gcc", 1)
            for r in range(NCORE):
                P.emit("Y", lambda e, r=r: e.dma_start(
                    out=kT_all[0:R, r * TC:(r + 1) * TC],
                    in_=cck_out.ap()[r * R:(r + 1) * R, :]), inc="ykt", inc_n=16)
            pes.close()
            vbuf0 = sbt("vbuf0", [128, 4 * D], BF16)
            vbuf1 = sbt("vbuf1", [128, 4 * D], BF16)
            Pb0 = sbt("Pb0", [128, 512], BF16)
            Pb1 = sbt("Pb1", [128, 512], BF16)
            Pb2 = sbt("Pb2", [128, 512], BF16)
            Pb3 = sbt("Pb3", [128, 512], BF16)
            PTb0 = sbt("PTb0", [128, 512], BF16)
            PTb1 = sbt("PTb1", [128, 512], BF16)
            vbuf = [vbuf0, vbuf1]
            Pb = [Pb0, Pb1, Pb2, Pb3]
            PTb = [PTb0, PTb1]

            P.wait("T", "ykt")
            nblocks = NT // QB

            def scoreA(tt, u, bank):
                P.emit("T", lambda e, tt=tt, u=u, bank=bank: e.matmul(
                    ps_sc[bank][:, :], qT_sb[0:R, tt * 128:(tt + 1) * 128],
                    kT_all[0:R, u * 512:(u + 1) * 512],
                    start=True, stop=True), inc="sca")
                P.wait("V", "sca")
                P.emit("V", lambda e, tt=tt, u=u, bank=bank: e.reduce_max(
                    mbufA[:, tt * NU + u: tt * NU + u + 1], ps_sc[bank][:, :],
                    axis=AX.X), inc="red")
                return P.cnt["red"]

            redm = {}
            for u in range(NU):
                for j in range(QB):
                    if u >= 1:
                        P.wait("T", "red", redm[(j, u - 1)])
                    redm[(j, u)] = scoreA(j, u, j)
            for j in range(QB):
                P.wait("V", "red")
                P.emit("V", lambda e, j=j: e.reduce_max(
                    st(NGMA)[:, j:j + 1], mbufA[:, j * NU:(j + 1) * NU],
                    axis=AX.X, negate=True), inc="vngma")

            vstream = 0
            vfree = []
            for qb in range(nblocks):
                tiles = [qb * QB + j for j in range(QB)]
                nxt = [qb * QB + QB + j for j in range(QB)] if qb + 1 < nblocks else None
                P.wait("T", "red")
                P.wait("T", "ptr")
                P.wait("T", "cex")
                P.wait("T", "vptc")
                P.wait("T", "vevat")
                P.wait("C", "vngma")
                P.wait("C", "ptr")
                scbm = {}
                cexm = {}
                ptrm = {}
                vptcm = {}
                mmam = {}
                redm = {}
                vloadm = {}
                for u in range(NU + 1):
                    if nxt is not None and u >= 1:
                        for j in range(QB):
                            P.wait("T", "cex", cexm[(j, u - 1)])
                            redm[(j, u - 1)] = scoreA(nxt[j], u - 1, j)
                    if u >= 1:
                        up = u - 1
                        for j, t in enumerate(tiles):
                            if up >= 1:
                                P.wait("T", "vptc", vptcm[(j, up - 1)])
                            P.wait("T", "cex", cexm[(j, up)])
                            pslot = (up % 2) * 2 + j
                            for us in range(4):
                                P.emit("T", lambda e, pslot=pslot, us=us, j=j: e.transpose(
                                    ps_pt[j][:, us * 128:(us + 1) * 128],
                                    Pb[pslot][:, us * 128:(us + 1) * 128], idbf_sb[:, :]),
                                    inc=("ptr" if us == 3 else None))
                            ptrm[(j, up)] = P.cnt["ptr"]
                            P.wait("V", "ptr", ptrm[(j, up)])
                            if up >= 1:
                                P.wait("V", "mma", mmam[(j, up - 1)])
                            P.emit("V", lambda e, j=j: e.tensor_copy(
                                PTb[j][:, :], ps_pt[j][:, :]), inc="vptc")
                            vptcm[(j, up)] = P.cnt["vptc"]
                    if u < NU:
                        vp = vstream % 2
                        if vstream >= 2:
                            P.wait("Y", vfree[vstream - 2][0], vfree[vstream - 2][1])
                        P.wait("Y", "gccq", (u % 4) + 1)
                        P.emit("Y", lambda e, u=u, vp=vp: e.dma_start(
                            out=vbuf[vp][:, :].rearrange("p (s d) -> p s d", s=4),
                            in_=ccq_out[u % 4].ap()[(u // 4) * 512:(u // 4 + 1) * 512, :]
                            .rearrange("(s p) d -> p s d", p=128)), inc=f"yv{vp}", inc_n=16)
                        vloadm[u] = (f"yv{vp}", P.cnt[f"yv{vp}"], vp)
                        vstream += 1
                        for j, t in enumerate(tiles):
                            if u >= 1:
                                if nxt is not None:
                                    P.wait("T", "red", redm[(j, u - 1)])
                                else:
                                    P.wait("T", "cex", cexm[(j, u - 1)])
                            P.emit("T", lambda e, t=t, u=u, j=j: e.matmul(
                                ps_sc[j][:, :], qT_sb[0:R, t * 128:(t + 1) * 128],
                                kT_all[0:R, u * 512:(u + 1) * 512],
                                start=True, stop=True), inc="scb")
                            scbm[(j, u)] = P.cnt["scb"]
                            P.wait("C", "scb", scbm[(j, u)])
                            if u >= 2:
                                P.wait("C", "ptr", ptrm[(j, u - 1)])
                            pslot = (u % 2) * 2 + j
                            P.emit("C", lambda e, t=t, u=u, j=j, pslot=pslot: e.activation(
                                Pb[pslot][:, :], ps_sc[j][:, :], AF.Exp,
                                bias=st(NGMA)[:, t:t + 1], scale=1.0,
                                accum_out=lbuf[:, t * NU + u: t * NU + u + 1]),
                                inc="cex")
                            cexm[(j, u)] = P.cnt["cex"]
                    if u >= 1:
                        up = u - 1
                        for j, t in enumerate(tiles):
                            P.wait("T", "vptc", vptcm[(j, up)])
                            P.wait("T", vloadm[up][0], vloadm[up][1])
                            vp = vloadm[up][2]
                            for dh in range(2):
                                for us in range(4):
                                    fin = (us == 3 and dh == 1)
                                    P.emit("T", lambda e, t=t, us=us, dh=dh, j=j, vp=vp,
                                           up=up: e.matmul(
                                        ps_at[t % 2][:, dh * 512:(dh + 1) * 512],
                                        PTb[j][:, us * 128:(us + 1) * 128],
                                        vbuf[vp][:, us * D + dh * 512: us * D + (dh + 1) * 512],
                                        start=(up == 0 and us == 0),
                                        stop=(up == NU - 1 and us == 3)),
                                        inc=("mma" if fin else None))
                            mmam[(j, up)] = P.cnt["mma"]
                        vfree.append(("mma", mmam[(QB - 1, up)]))
                if nxt is not None:
                    for j in range(QB):
                        P.wait("T", "cex", cexm[(j, NU - 1)])
                        redm[(j, NU - 1)] = scoreA(nxt[j], NU - 1, j)
                if nxt is not None:
                    for j in range(QB):
                        P.wait("V", "red", redm[(j, NU - 1)])
                        P.emit("V", lambda e, j=j, nt_=nxt[j]: e.reduce_max(
                            st(NGMA)[:, nt_:nt_ + 1], mbufA[:, nt_ * NU:(nt_ + 1) * NU],
                            axis=AX.X, negate=True), inc="vngma")
                for j, t in enumerate(tiles):
                    P.wait("V", "cex", cexm[(j, NU - 1)])
                    P.emit("V", lambda e, t=t: e.reduce_sum(
                        st(LSUM)[:, t:t + 1], lbuf[:, t * NU:(t + 1) * NU], axis=AX.X),
                        inc="vch")
                    P.wait("V", "vch")
                    P.emit("V", lambda e, t=t: e.reciprocal(
                        st(RECL)[:, t:t + 1], st(LSUM)[:, t:t + 1]), inc="vch")
                    P.wait("V", "vch")
                    P.wait("V", "mma", mmam[(j, NU - 1)])
                    P.emit("V", lambda e, t=t: e.scalar_tensor_tensor(
                        x_sb[:, t * D:(t + 1) * D], ps_at[t % 2][:, :],
                        st(RECL)[:, t:t + 1], x_sb[:, t * D:(t + 1) * D],
                        ALU.mult, ALU.add), inc="vevat")

    P.wait("C", "s_x", 16)
    P.wait("V", "s_x", 16)
    P.wait("T", "s_const")
    emit_b1 = True
    if emit_b1:
        import contextlib as _ctx
        with _ctx.ExitStack() as bes:
            ps_b1a = bes.enter_context(nc.psum_tensor("ps_b1a", [128, 512], F32))
            ps_b1b = bes.enter_context(nc.psum_tensor("ps_b1b", [128, 512], F32))
            b1nat = bes.enter_context(nc.sbuf_tensor("b1nat", [E, H], F32))
            P.emit("Y", lambda e: e.dma_start(out=b1nat[:, :], in_=b1_t),
                   inc="s_b1", inc_n=16)
            P.wait("T", "s_b1", 16)
            ps_b1 = [ps_b1a, ps_b1b]
            P.wait("T", "s_const")
            P.emit("T", lambda e: e.matmul(
                ps_b1a[:, 0:E], onesf_sb[0:1, :], bgr_sb[0:1, 0:E],
                start=True, stop=True), inc="b1tr")
            P.wait("V", "b1tr")
            P.emit("V", lambda e: e.tensor_copy(bgb_sb[:, :], ps_b1a[:, 0:E]),
                   inc="b1cp")
            for hc in range(NHC):
                par = hc % 2
                P.wait("T", "b1cp", 1 + max(0, hc - 1))
                P.emit("T", lambda e, hc=hc, par=par: e.transpose(
                    ps_b1[par][0:128, 0:E], b1nat[0:E, hc * 128:(hc + 1) * 128],
                    idf_sb[0:E, 0:E]), inc="b1tr")
                P.wait("V", "b1tr")
                P.emit("V", lambda e, hc=hc, par=par: e.tensor_copy(
                    b1T[:, hc * E:(hc + 1) * E], ps_b1[par][0:128, 0:E]), inc="b1cp")

    emit_ffn_phase()
    P.gate()
    emit_attn_phase()
    P.gate()
    emit_ffn_phase()
    P.gate(("Y",))
    P.emit("Y", lambda e: e.dma_start(
        out=x_out.rearrange("(t p) d -> p t d", p=128),
        in_=x_sb[:, :].rearrange("p (t d) -> p t d", d=D)),
        inc="s_out", inc_n=16)
    P.wait("Y", "s_out", 16)

    with nc.Block() as block:
        @block.tensor
        def _(t):
            P.replay("T", t)

        @block.vector
        def _(v):
            P.replay("V", v)

        @block.scalar
        def _(c):
            P.replay("C", c)

        @block.gpsimd
        def _(g):
            P.replay("G", g)

        @block.sync
        def _(y):
            P.replay("Y", y)

    est.close()
    return nc


def kernel(L0_attn_x, L0_attn_Q, L0_attn_K, L0_attn_VO, L0_ffn_Wg, L0_ffn_bg,
           L0_ffn_W1, L0_ffn_b1, L0_ffn_W2, L0_ffn_b2):
    cfg = CFG
    NCORE, TC, D = cfg["NCORE"], cfg["TC"], cfg["D"]
    x = np.ascontiguousarray(np.asarray(L0_attn_x, dtype=np.float32).reshape(NCORE * TC, D))
    common = {
        "attn_Q": np.ascontiguousarray(np.asarray(L0_attn_Q, dtype=np.float32)),
        "attn_K": np.ascontiguousarray(np.asarray(L0_attn_K, dtype=np.float32)),
        "attn_VO": np.ascontiguousarray(np.asarray(L0_attn_VO, dtype=np.float32)),
        "ffn_Wg": np.ascontiguousarray(np.asarray(L0_ffn_Wg, dtype=np.float32)),
        "ffn_bg": np.ascontiguousarray(np.asarray(L0_ffn_bg, dtype=np.float32)),
        "ffn_W1": np.ascontiguousarray(np.asarray(L0_ffn_W1, dtype=np.float32)),
        "ffn_b1": np.ascontiguousarray(np.asarray(L0_ffn_b1, dtype=np.float32)),
        "ffn_W2": np.ascontiguousarray(np.asarray(L0_ffn_W2, dtype=np.float32)),
        "ffn_b2": np.ascontiguousarray(np.asarray(L0_ffn_b2, dtype=np.float32)),
    }
    nc = build_program(cfg)
    in_maps = []
    for c in range(NCORE):
        m = dict(common)
        m["x_in"] = np.ascontiguousarray(x[c * TC:(c + 1) * TC])
        in_maps.append(m)
    res = run_bass_kernel_spmd(nc, in_maps, core_ids=list(range(NCORE)))
    out = np.concatenate([res.results[c]["x_outp"] for c in range(NCORE)], axis=0)
    return out.reshape(1, NCORE * TC, D)


# version 14
# speedup vs baseline: 1.0123x; 1.0123x over previous
import numpy as np
import ml_dtypes
import concourse.bass as bass
import concourse.mybir as mybir
from concourse.bass_utils import run_bass_kernel_spmd

F32 = mybir.dt.float32
BF16 = mybir.dt.bfloat16
U32 = mybir.dt.uint32
AF = mybir.ActivationFunctionType
ALU = mybir.AluOpType
AX = mybir.AxisListType

CFG = dict(D=1024, H=4096, E=8, TC=2048, R=64, NCORE=8)


class Prog:

    ENGS = ("T", "V", "C", "G", "Y")

    def __init__(self, nc):
        self.nc = nc
        self.ops = {e: [] for e in self.ENGS}
        self.sems = {}
        self.cnt = {}
        self.last_gate = {e: {} for e in self.ENGS}

    def sem(self, name):
        if name not in self.sems:
            self.sems[name] = self.nc.alloc_semaphore(name)
            self.cnt[name] = 0
        return self.sems[name]

    def wait(self, eng, name, val=None):
        h = self.sem(name)
        v = self.cnt[name] if val is None else val
        if v > 0:
            self.ops[eng].append(("w", h, v))

    def emit(self, eng, fn, inc=None, inc_n=1):
        h = None
        if inc is not None:
            h = self.sem(inc)
            self.cnt[inc] += inc_n
        self.ops[eng].append(("o", fn, h, inc_n))

    def gate(self, engs=None):
        for e in (engs or self.ENGS):
            for name, v in self.cnt.items():
                if v > 0 and self.last_gate[e].get(name, 0) < v:
                    self.ops[e].append(("w", self.sems[name], v))
                    self.last_gate[e][name] = v

    def replay(self, eng_key, eng_obj):
        for item in self.ops[eng_key]:
            if item[0] == "w":
                eng_obj.wait_ge(item[1], item[2])
            else:
                ins = item[1](eng_obj)
                if item[2] is not None:
                    ins.then_inc(item[2], item[3])


def build_program(cfg):
    D, H, E, TC, R, NCORE = (cfg[k] for k in ("D", "H", "E", "TC", "R", "NCORE"))
    Dk = D // 128
    NT = TC // 128
    NG = TC // 512
    HQG = 1024 if H >= 1024 else 512
    HQ = H // HQG
    Hc = HQG // 128
    U = NCORE * TC
    NU = U // 512
    DR = D + R
    QB = 2
    NHC = H // 128
    RSC = float(D) ** -0.5

    nc = bass.Bass("TRN2", target_bir_lowering=False, dynamic_dma_scratch_size=16384)

    x_in = nc.dram_tensor("x_in", [TC, D], F32, kind="ExternalInput").ap()
    Wq_t = nc.dram_tensor("attn_Q", [R, D], F32, kind="ExternalInput").ap()
    Wk_t = nc.dram_tensor("attn_K", [R, D], F32, kind="ExternalInput").ap()
    Wvo_t = nc.dram_tensor("attn_VO", [D, D], F32, kind="ExternalInput").ap()
    Wg_t = nc.dram_tensor("ffn_Wg", [D, E], F32, kind="ExternalInput").ap()
    bg_t = nc.dram_tensor("ffn_bg", [E], F32, kind="ExternalInput").ap()
    W1_t = nc.dram_tensor("ffn_W1", [E, D, H], F32, kind="ExternalInput").ap()
    b1_t = nc.dram_tensor("ffn_b1", [E, H], F32, kind="ExternalInput").ap()
    W2_t = nc.dram_tensor("ffn_W2", [E, H, D], F32, kind="ExternalInput").ap()
    b2_t = nc.dram_tensor("ffn_b2", [E, D], F32, kind="ExternalInput").ap()
    x_out = nc.dram_tensor("x_outp", [TC, D], F32, kind="ExternalOutput").ap()

    cc_in = nc.dram_tensor("cc_in", [TC, D], BF16, kind="Internal")
    ccq_out = [nc.dram_tensor(f"ccq{j}", [NCORE * 512, D], BF16, kind="Internal")
               for j in range(4)]
    cck_in = nc.dram_tensor("cck_in", [R, TC], BF16, kind="Internal")
    cck_out = nc.dram_tensor("cck_out", [R * NCORE, TC], BF16, kind="Internal")

    id_bf = nc.inline_tensor(np.eye(128, dtype=ml_dtypes.bfloat16), "id_bf").ap()
    id_f32 = nc.inline_tensor(np.eye(128, dtype=np.float32), "id_f32").ap()
    ones_bf = nc.inline_tensor(np.ones((1, 128), dtype=ml_dtypes.bfloat16), "ones_bf").ap()
    iota_e = nc.inline_tensor(
        np.tile(np.arange(E, dtype=np.float32), (128, 1)), "iota_e").ap()
    ones_f = nc.inline_tensor(np.ones((1, 128), dtype=np.float32), "ones_f").ap()

    P = Prog(nc)

    import contextlib
    est = contextlib.ExitStack()
    sb = lambda name, shape, dt: est.enter_context(nc.sbuf_tensor(name, shape, dt))

    x_sb = sb("x_sb", [128, NT * D], F32)
    lnT = sb("lnT", [128, Dk * TC], BF16)
    idbf_sb = sb("idbf_sb", [128, 128], BF16)
    idf_sb = sb("idf_sb", [128, 128], F32)
    onesbf_sb = sb("onesbf_sb", [1, 128], BF16)
    onesf_sb = sb("onesf_sb", [1, 128], F32)
    bgr_sb = sb("bgr_sb", [1, E], F32)
    bgb_sb = sb("bgb_sb", [128, E], F32)
    iotae_sb = sb("iotae_sb", [128, E], F32)
    wg_sb = sb("wg_sb", [128, Dk * E], BF16)
    b1T = sb("b1T", [128, NHC * E], F32)
    b2_sb = sb("b2_sb", [E, D], BF16)
    lnbf = [sb("lnbf0", [128, D], BF16), sb("lnbf1", [128, D], BF16)]
    NSTAT = 7
    stat = sb("stat", [128, NSTAT * NT], F32)
    _sq = sb("sq_scr0", [128, D], F32)
    sq_scr = [_sq, _sq]
    g_sb = sb("g_sb", [128, NT * E], F32)

    SUMS, SUMSQ, NEGM, EX2, M2, VAR, RSTD = range(7)
    NGMG, SSUM, IDXF, HALFW = 0, 1, 2, 3
    NGMA, LSUM, RECL = 3 + 1, 3 + 2, 3 + 3

    def st(i):
        return stat[:, i * NT:(i + 1) * NT]

    P.emit("Y", lambda e: e.dma_start(
        out=x_sb[:, :].rearrange("p (t d) -> p t d", d=D),
        in_=x_in.rearrange("(t p) d -> p t d", p=128)),
        inc="s_x", inc_n=16)
    for dst, src in ((idbf_sb, id_bf), (idf_sb, id_f32),
                     (onesbf_sb, ones_bf), (onesf_sb, ones_f), (iotae_sb, iota_e)):
        P.emit("Y", lambda e, d=dst, s=src: e.dma_start(out=d[:, :], in_=s),
               inc="s_const", inc_n=16)
    P.emit("Y", lambda e: e.dma_start(out=bgr_sb[:, :], in_=bg_t.rearrange("(o e) -> o e", o=1)),
           inc="s_const", inc_n=16)
    P.emit("G", lambda e: e.dma_start(
        out=wg_sb[:, :].rearrange("p (c e) -> p c e", e=E),
        in_=Wg_t.rearrange("(c p) e -> p c e", p=128)),
        inc="s_wg", inc_n=16)
    P.emit("G", lambda e: e.dma_start(out=b2_sb[:, :], in_=b2_t), inc="s_b2", inc_n=16)

    def emit_ln(ps_bf, ps_f):
        cln0 = P.cnt.get("cln", 0)
        for t in range(NT):
            if t >= 1:
                P.wait("C", "cln", cln0 + t)
            P.emit("C", lambda e, t=t: e.activation(
                sq_scr[t % 2][:, :], x_sb[:, t * D:(t + 1) * D], AF.Square,
                accum_out=st(SUMSQ)[:, t:t + 1]), inc="cln")
            P.emit("V", lambda e, t=t: e.reduce_sum(
                st(SUMS)[:, t:t + 1], x_sb[:, t * D:(t + 1) * D], axis=AX.X),
                inc="vch")
        P.wait("V", "vch")
        P.emit("V", lambda e: e.tensor_scalar(
            st(NEGM), st(SUMS), -1.0 / D, None, ALU.mult), inc="vch")
        P.wait("V", "cln")
        P.emit("V", lambda e: e.tensor_scalar(
            st(EX2), st(SUMSQ), 1.0 / D, None, ALU.mult), inc="vch")
        P.wait("V", "vch")
        P.emit("V", lambda e: e.tensor_mul(st(M2), st(NEGM), st(NEGM)), inc="vch")
        P.wait("V", "vch")
        P.emit("V", lambda e: e.tensor_sub(st(VAR), st(EX2), st(M2)), inc="vch")
        P.wait("V", "vch")
        P.emit("V", lambda e: e.tensor_scalar(
            st(VAR), st(VAR), 1e-5, None, ALU.add), inc="vvar")
        P.wait("C", "vvar")
        P.emit("C", lambda e: e.activation(
            st(RSTD), st(VAR), AF.Sqrt, scale=1.0), inc="csq")
        P.wait("V", "csq")
        P.emit("V", lambda e: e.reciprocal(st(RSTD), st(RSTD)), inc="vrs")
        P.wait("V", "vrs")
        ttr0 = P.cnt["ttr"] if "ttr" in P.cnt else 0
        vcp0 = P.cnt["vcp"] if "vcp" in P.cnt else 0
        for t in range(NT):
            par = t % 2
            if t >= 2:
                P.wait("V", "ttr", ttr0 + Dk * (t - 1))
            P.emit("V", lambda e, t=t, par=par: e.tensor_scalar(
                lnbf[par][:, :], x_sb[:, t * D:(t + 1) * D],
                st(NEGM)[:, t:t + 1], st(RSTD)[:, t:t + 1], ALU.add, ALU.mult),
                inc="vln")
            P.wait("T", "vln")
            for c in range(Dk):
                n = t * Dk + c
                cp = n % 2
                if n >= 2:
                    P.wait("T", "vcp", vcp0 + n - 1)
                P.emit("T", lambda e, t=t, par=par, c=c, cp=cp: e.transpose(
                    ps_bf[cp][:, 0:128], lnbf[par][:, c * 128:(c + 1) * 128],
                    idbf_sb[:, :]), inc="ttr")
                P.wait("V", "ttr")
                P.emit("V", lambda e, t=t, c=c, cp=cp: e.tensor_copy(
                    lnT[:, c * TC + t * 128: c * TC + (t + 1) * 128],
                    ps_bf[cp][:, 0:128]), inc="vcp")

    def emit_gate(ps_f, maskT, maxv8, idx8, mask_scr):
        P.wait("T", "s_wg", 16)
        cg0 = P.cnt.get("vg", 0)
        for t in range(NT):
            par = t % 2
            if t >= 2:
                P.wait("T", "vg", cg0 + t - 1)
            for c in range(Dk):
                P.emit("T", lambda e, t=t, c=c, par=par: e.matmul(
                    ps_f[par][:, 0:E], lnT[:, c * TC + t * 128: c * TC + (t + 1) * 128],
                    wg_sb[:, c * E:(c + 1) * E],
                    start=(c == 0), stop=(c == Dk - 1)),
                    inc=("tg" if c == Dk - 1 else None))
            P.wait("V", "tg")
            P.emit("V", lambda e, t=t, par=par: e.tensor_add(
                g_sb[:, t * E:(t + 1) * E], ps_f[par][:, 0:E], bgb_sb[:, :]),
                inc="vg")
        P.wait("V", "vg")
        for t in range(NT):
            P.emit("V", lambda e, t=t: e.reduce_max(
                st(NGMG)[:, t:t + 1], g_sb[:, t * E:(t + 1) * E], axis=AX.X,
                negate=True), inc="vngm")
        P.wait("C", "vngm")
        cexp0 = P.cnt.get("cexp", 0)
        for t in range(NT):
            if t >= 1:
                P.wait("C", "cexp", cexp0 + t)
            P.emit("C", lambda e, t=t: e.activation(
                maxv8[:, 0:E], g_sb[:, t * E:(t + 1) * E], AF.Exp,
                bias=st(NGMG)[:, t:t + 1], scale=1.0,
                accum_out=st(SSUM)[:, t:t + 1]), inc="cexp")
        P.wait("V", "cexp")
        P.emit("V", lambda e: e.tensor_scalar(
            st(SSUM), st(SSUM), 2.0, None, ALU.mult), inc="vch")
        P.wait("V", "vch")
        P.emit("V", lambda e: e.reciprocal(st(HALFW), st(SSUM)), inc="vch")
        P.wait("V", "vch")
        vmt0 = P.cnt.get("vmt", 0)
        for t in range(NT):
            par8 = (t % 2) * 8
            if t >= 2:
                P.wait("V", "vch")
            P.emit("V", lambda e, t=t, par8=par8: e.max(
                maxv8[:, par8:par8 + 8], g_sb[:, t * E:(t + 1) * E]), inc="vchA")
            P.wait("V", "vchA")
            P.emit("V", lambda e, t=t, par8=par8: e.max_index(
                idx8[:, par8:par8 + 8], maxv8[:, par8:par8 + 8],
                g_sb[:, t * E:(t + 1) * E]), inc="vchB")
            P.wait("V", "vchB")
            P.emit("V", lambda e, t=t, par8=par8: e.tensor_copy(
                st(IDXF)[:, t:t + 1], idx8[:, par8:par8 + 1]), inc="vch")
        P.wait("V", "vch")
        tmt0 = P.cnt.get("tmt", 0)
        for t in range(NT):
            par = t % 2
            if t >= 2:
                P.wait("V", "tmt", tmt0 + t - 1)
            P.emit("V", lambda e, t=t, par=par: e.tensor_scalar(
                mask_scr[:, par * E:(par + 1) * E], iotae_sb[:, :], st(IDXF)[:, t:t + 1],
                st(HALFW)[:, t:t + 1], ALU.is_equal, ALU.mult), inc="vmk")
            P.wait("T", "vmk")
            if t >= 2:
                P.wait("T", "vmt", vmt0 + t - 1)
            P.emit("T", lambda e, t=t, par=par: e.transpose(
                ps_f[par][0:E, 0:128], mask_scr[:, par * E:(par + 1) * E], idf_sb[:, :]),
                inc="tmt")
            P.wait("V", "tmt")
            P.emit("V", lambda e, t=t, par=par: e.tensor_copy(
                maskT[0:E, t * 128:(t + 1) * 128], ps_f[par][0:E, 0:128]),
                inc="vmt")

    def emit_ffn_main(ps_h, ps_o, ps_x, w1b, w2b, hs, mh, mbc, mrow, maskT,
                      add_bias, store_out=False):
        mm1_0 = P.cnt.get("mm1", 0)
        sil0 = P.cnt.get("sil", 0)
        msk0 = P.cnt.get("msk", 0)
        mm2_0 = P.cnt.get("mm2", 0)
        ev0 = P.cnt.get("evac", 0)
        tbc0 = P.cnt.get("tbc", 0)
        cbc0 = P.cnt.get("cbc", 0)
        n_mm1 = 0
        n_o = 0

        def build_mb(bi):
            bpar = bi % 2
            if bi >= 1:
                P.wait("G", "tbc", tbc0 + bi * NG)
            P.wait("G", "vmt")
            P.emit("G", lambda e, bi=bi: e.dma_start(
                out=mrow[0:1, :], in_=maskT[bi:bi + 1, :]),
                inc=f"gmr{bi % 2}", inc_n=16)
            for gch in range(NG):
                n = bi * NG + gch
                xp = n % 2
                P.wait("T", f"gmr{bpar}")
                if n >= 2:
                    P.wait("T", "cbc", cbc0 + n - 1)
                P.emit("T", lambda e, gch=gch, xp=xp: e.matmul(
                    ps_x[xp][:, :], onesbf_sb[0:1, :],
                    mrow[0:1, gch * 512: (gch + 1) * 512],
                    start=True, stop=True), inc="tbc")
                P.wait("C", "tbc")
                P.wait("C", "msk")
                P.emit("C", lambda e, bpar=bpar, gch=gch, xp=xp: e.activation(
                    mbc[bpar][:, gch * 512:(gch + 1) * 512],
                    ps_x[xp][:, :], AF.Copy), inc="cbc")

        for pre in range(2):
            pei, phq = pre // HQ, pre % HQ
            P.emit("G", lambda e, pei=pei, phq=phq, pre=pre: e.dma_start(
                out=w1b[pre][:, :].rearrange("p (c h) -> p c h", h=HQG),
                in_=W1_t[pei, :, phq * HQG:(phq + 1) * HQG]
                .rearrange("(c p) h -> p c h", p=128)), inc=f"wld{pre}", inc_n=16)
            P.emit("G", lambda e, pei=pei, phq=phq, pre=pre: e.dma_start(
                out=w2b[pre][:, :].rearrange("p (s d) -> p s d", s=Hc),
                in_=W2_t[pei, phq * HQG:(phq + 1) * HQG, :]
                .rearrange("(s p) d -> p s d", p=128)), inc=f"wld{pre}", inc_n=16)
        build_mb(0)
        sched = []
        for it, (ei, hq) in enumerate([(a, b) for a in range(E) for b in range(HQ)]):
            for g in range(NG):
                sched.append((it, ei, hq, g))
        state = {"n_mm1": 0, "n_o": 0}

        def emit_mm2(gi):
            it, ei, hq, g = sched[gi]
            par = it % 2
            mhp = gi % 2
            seed = add_bias and it == 0
            P.wait("T", "msk", msk0 + 4 * (gi + 1))
            for ts in range(4):
                n_o = state["n_o"]
                op = n_o % 2
                tglob = g * 4 + ts
                if n_o >= 2:
                    P.wait("T", "evac", ev0 + n_o - 1)
                for dh in range(2):
                    if seed:
                        P.emit("T", lambda e, op=op, tglob=tglob, dh=dh: e.matmul(
                            ps_o[op][:, dh * 512:(dh + 1) * 512],
                            maskT[0:E, tglob * 128:(tglob + 1) * 128],
                            b2_sb[0:E, dh * 512:(dh + 1) * 512],
                            start=True, stop=False))
                    for h1 in range(Hc):
                        fin = (h1 == Hc - 1 and dh == 1)
                        P.emit("T", lambda e, op=op, mhp=mhp, h1=h1, ts=ts, dh=dh, par=par,
                               seed=seed: e.matmul(
                            ps_o[op][:, dh * 512:(dh + 1) * 512],
                            mh[mhp][:, h1 * 512 + ts * 128: h1 * 512 + ts * 128 + 128],
                            w2b[par][:, h1 * 1024 + dh * 512: h1 * 1024 + (dh + 1) * 512],
                            start=(h1 == 0 and not seed), stop=(h1 == Hc - 1)),
                            inc=("mm2" if fin else None))
                P.wait("V", "mm2")
                if n_o >= NG * 4:
                    P.wait("V", "evac", ev0 + n_o - NG * 4 + 1)
                P.emit("V", lambda e, op=op, tglob=tglob: e.tensor_add(
                    x_sb[:, tglob * D:(tglob + 1) * D],
                    x_sb[:, tglob * D:(tglob + 1) * D], ps_o[op][:, :]),
                    inc="evac")
                state["n_o"] += 1
                if store_out and it == E * HQ - 1:
                    P.wait("Y", "evac", P.cnt["evac"])
                    P.emit("Y", lambda e, tglob=tglob: e.dma_start(
                        out=x_out.rearrange("(t p) d -> t p d", p=128)[tglob],
                        in_=x_sb[:, tglob * D:(tglob + 1) * D]),
                        inc="s_out", inc_n=16)

        for gi, (it, ei, hq, g) in enumerate(sched):
            par = it % 2
            if g == 0:
                if hq == 0 and ei + 1 < E:
                    build_mb(ei + 1)
                if it >= 2:
                    P.wait("G", "mm2", mm2_0 + (it - 1) * NG * 4)
                    P.emit("G", lambda e, ei=ei, hq=hq, par=par: e.dma_start(
                        out=w1b[par][:, :].rearrange("p (c h) -> p c h", h=HQG),
                        in_=W1_t[ei, :, hq * HQG:(hq + 1) * HQG]
                        .rearrange("(c p) h -> p c h", p=128)), inc=f"wld{par}", inc_n=16)
                    P.emit("G", lambda e, ei=ei, hq=hq, par=par: e.dma_start(
                        out=w2b[par][:, :].rearrange("p (s d) -> p s d", s=Hc),
                        in_=W2_t[ei, hq * HQG:(hq + 1) * HQG, :]
                        .rearrange("(s p) d -> p s d", p=128)), inc=f"wld{par}", inc_n=16)
                P.wait("T", f"wld{par}")
                if hq == 0:
                    P.wait("V", "cbc", cbc0 + (ei + 1) * NG)
            mhp = gi % 2
            for h1 in range(Hc):
                n_mm1 = state["n_mm1"]
                hp = n_mm1 % 2
                if n_mm1 >= 2:
                    P.wait("T", "sil", sil0 + n_mm1 - 1)
                for c in range(Dk):
                    P.emit("T", lambda e, par=par, c=c, h1=h1, g=g, hp=hp: e.matmul(
                        ps_h[hp][:, :],
                        w1b[par][:, c * HQG + h1 * 128: c * HQG + (h1 + 1) * 128],
                        lnT[:, c * TC + g * 512: c * TC + (g + 1) * 512],
                        start=(c == 0), stop=(c == Dk - 1)),
                        inc=("mm1" if c == Dk - 1 else None))
                P.wait("C", "mm1")
                if n_mm1 >= 2:
                    P.wait("C", "msk", msk0 + n_mm1 - 1)
                hsp = n_mm1 % 2
                bidx = (hq * Hc + h1) * E + ei
                P.emit("C", lambda e, hp=hp, hsp=hsp, bidx=bidx: e.activation(
                    hs[hsp][:, :], ps_h[hp][:, :], AF.Silu,
                    bias=b1T[:, bidx:bidx + 1], scale=1.0), inc="sil")
                P.wait("V", "sil")
                if gi >= 2:
                    P.wait("V", "mm2", mm2_0 + 4 * (gi - 1))
                P.emit("V", lambda e, hsp=hsp, mhp=mhp, h1=h1, ei=ei, g=g: e.tensor_tensor(
                    mh[mhp][:, h1 * 512:(h1 + 1) * 512], hs[hsp][:, :],
                    mbc[ei % 2][:, g * 512:(g + 1) * 512],
                    ALU.mult), inc="msk")
                state["n_mm1"] += 1
            if gi >= 1:
                emit_mm2(gi - 1)
        emit_mm2(len(sched) - 1)

    _phase_ctr = [0]

    def emit_ffn_phase(store_out=False):
        import contextlib as _ctx
        _phase_ctr[0] += 1
        sfx = f"_f{_phase_ctr[0]}"
        with _ctx.ExitStack() as fes:
            pst = lambda n, sh, dt: fes.enter_context(nc.psum_tensor(n + sfx, sh, dt))
            sbt = lambda n, sh, dt: fes.enter_context(nc.sbuf_tensor(n + sfx, sh, dt))
            ps_h0 = pst("ps_h0", [128, 512], F32)
            ps_h1 = pst("ps_h1", [128, 512], F32)
            ps_o0 = pst("ps_o0", [128, 1024], F32)
            ps_o1 = pst("ps_o1", [128, 1024], F32)
            w1b0 = sbt("w1b0", [128, Dk * HQG], BF16)
            w1b1 = sbt("w1b1", [128, Dk * HQG], BF16)
            w2b0 = sbt("w2b0", [128, Hc * D], BF16)
            w2b1 = sbt("w2b1", [128, Hc * D], BF16)
            hs0 = sbt("hs0", [128, 512], BF16)
            hs1 = sbt("hs1", [128, 512], BF16)
            mh0 = sbt("mh0", [128, Hc * 512], BF16)
            mh1 = sbt("mh1", [128, Hc * 512], BF16)
            mbc0 = sbt("mbc0", [128, TC], BF16)
            mbc1 = sbt("mbc1", [128, TC], BF16)
            maskT = sbt("maskT", [E, TC], BF16)
            mrow = sbt("mrow", [1, TC], BF16)
            maxv8 = sbt("maxv8", [128, 8 * 2], F32)
            idx8 = sbt("idx8", [128, 8 * 2], mybir.dt.uint16)
            mask_scr = sbt("mask_scr", [128, E * 2], F32)
            ps_h = [ps_h0, ps_h1]
            ps_o = [ps_o0, ps_o1]
            P.gate()
            with _ctx.ExitStack() as lns:
                ps_bf = [lns.enter_context(nc.psum_tensor("ps_bfa" + sfx, [128, 512], BF16)),
                         lns.enter_context(nc.psum_tensor("ps_bfb" + sfx, [128, 512], BF16))]
                emit_ln(ps_bf, ps_h)
            P.gate(("T", "C", "V"))
            emit_gate(ps_h, maskT, maxv8, idx8, mask_scr)
            P.gate(("T", "C", "V", "G"))
            ps_x = [fes.enter_context(nc.psum_tensor("ps_x0" + sfx, [128, 512], F32)),
                    fes.enter_context(nc.psum_tensor("ps_x1" + sfx, [128, 512], F32))]
            emit_ffn_main(ps_h, ps_o, ps_x, [w1b0, w1b1], [w2b0, w2b1],
                          [hs0, hs1], [mh0, mh1], [mbc0, mbc1], mrow, maskT,
                          add_bias=True, store_out=store_out)

    def emit_attn_phase():
        import contextlib as _ctx
        with _ctx.ExitStack() as aes:
            pst = lambda n, sh, dt: aes.enter_context(nc.psum_tensor(n, sh, dt))
            sbt = lambda n, sh, dt: aes.enter_context(nc.sbuf_tensor(n, sh, dt))
            ps_sc0 = pst("ps_sc0", [128, 512], F32)
            ps_sc1 = pst("ps_sc1", [128, 512], F32)
            ps_at0 = pst("ps_at0", [128, 1024], F32)
            ps_at1 = pst("ps_at1", [128, 1024], F32)
            ps_pt0 = pst("ps_pt0", [128, 512], BF16)
            ps_pt1 = pst("ps_pt1", [128, 512], BF16)
            qT_sb = sbt("qT_sb", [R, TC], BF16)
            lbuf = sbt("lbuf", [128, NT * NU], F32)
            mbufA = sbt("mbufA", [128, NT * NU], F32)
            kT_all = sbt("kT_all", [R, U], BF16)
            vstg = [sbt("vstg0", [128, D], BF16), sbt("vstg1", [128, D], BF16)]
            kTc_sb = sbt("kTc_sb", [R, TC], BF16)
            ps_sc = [ps_sc0, ps_sc1]
            ps_at = [ps_at0, ps_at1]
            ps_pt = [ps_pt0, ps_pt1]
            pes = aes.enter_context(_ctx.ExitStack())
            qwb = pes.enter_context(nc.sbuf_tensor("qwb", [R, D], BF16))
            kwb = pes.enter_context(nc.sbuf_tensor("kwb", [R, D], BF16))
            vob = pes.enter_context(nc.sbuf_tensor("vob", [128, Dk * D], BF16))
            QTb = pes.enter_context(nc.sbuf_tensor("QTb", [128, Dk * R], BF16))
            KTb = pes.enter_context(nc.sbuf_tensor("KTb", [128, Dk * R], BF16))
            P.emit("G", lambda e: e.dma_start(out=qwb[:, :], in_=Wq_t), inc="aql", inc_n=16)
            P.emit("G", lambda e: e.dma_start(out=kwb[:, :], in_=Wk_t), inc="aql", inc_n=16)
            P.emit("G", lambda e: e.dma_start(
                out=vob[:, :].rearrange("p (c d) -> p c d", d=D),
                in_=Wvo_t.rearrange("(c p) d -> p c d", p=128)),
                inc="aql", inc_n=16)

            P.gate()
            emit_ln(ps_pt, ps_sc)
            P.gate(("T", "C", "V"))

            P.wait("T", "aql")
            for i, (wsrc, wdst) in enumerate(((qwb, QTb), (kwb, KTb))):
                for c in range(Dk):
                    n = i * Dk + c
                    pp = n % 2
                    if n >= 2:
                        P.wait("T", "vqt", n - 1)
                    P.emit("T", lambda e, wsrc=wsrc, c=c, pp=pp: e.transpose(
                        ps_pt[pp][:, 0:R], wsrc[0:R, c * 128:(c + 1) * 128],
                        idbf_sb[0:R, 0:R]), inc="tqt")
                    P.wait("V", "tqt")
                    P.emit("V", lambda e, wdst=wdst, c=c, pp=pp: e.tensor_copy(
                        wdst[:, c * R:(c + 1) * R], ps_pt[pp][:, 0:R]), inc="vqt")
            P.gate(("T", "V"))

            for tch in range(TC // 512):
                pp = tch % 2
                if tch >= 2:
                    P.wait("T", "cq", tch - 1)
                for c in range(Dk):
                    P.emit("T", lambda e, tch=tch, c=c, pp=pp: e.matmul(
                        ps_sc[pp][0:R, 0:512], QTb[:, c * R:(c + 1) * R],
                        lnT[:, c * TC + tch * 512: c * TC + (tch + 1) * 512],
                        start=(c == 0), stop=(c == Dk - 1)),
                        inc=("tq" if c == Dk - 1 else None))
                P.wait("C", "tq")
                P.emit("C", lambda e, tch=tch, pp=pp: e.activation(
                    qT_sb[0:R, tch * 512:(tch + 1) * 512], ps_sc[pp][0:R, 0:512],
                    AF.Copy, scale=RSC), inc="cq")
            P.gate(("T", "C"))

            for tch in range(TC // 512):
                pp = tch % 2
                if tch >= 2:
                    P.wait("T", "vkp", tch - 1)
                for c in range(Dk):
                    P.emit("T", lambda e, tch=tch, c=c, pp=pp: e.matmul(
                        ps_sc[pp][0:R, 0:512], KTb[:, c * R:(c + 1) * R],
                        lnT[:, c * TC + tch * 512: c * TC + (tch + 1) * 512],
                        start=(c == 0), stop=(c == Dk - 1)),
                        inc=("tkp" if c == Dk - 1 else None))
                P.wait("V", "tkp")
                P.emit("V", lambda e, tch=tch, pp=pp: e.tensor_copy(
                    kTc_sb[0:R, tch * 512:(tch + 1) * 512], ps_sc[pp][0:R, 0:512]),
                    inc="vkp")
            P.wait("Y", "vkp")
            P.emit("Y", lambda e: e.dma_start(
                out=cck_in.ap(), in_=kTc_sb[:, :]), inc="yccin", inc_n=16)
            P.wait("G", "yccin")
            P.emit("G", lambda e: e.collective_compute(
                "AllGather", ALU.bypass,
                replica_groups=[list(range(NCORE))],
                ins=[cck_in.ap().opt()], outs=[cck_out.ap().opt()]), inc="gcc", inc_n=1)
            cin_rows = cc_in.ap().rearrange("(t p) d -> t p d", p=128)
            for t in range(NT):
                pp = t % 2
                if t >= 2:
                    P.wait("T", "cvp", t - 1)
                for dh in range(2):
                    for c in range(Dk):
                        P.emit("T", lambda e, t=t, c=c, dh=dh, pp=pp: e.matmul(
                            ps_at[pp][:, dh * 512:(dh + 1) * 512],
                            lnT[:, c * TC + t * 128: c * TC + (t + 1) * 128],
                            vob[:, c * D + dh * 512: c * D + (dh + 1) * 512],
                            start=(c == 0), stop=(c == Dk - 1)),
                            inc=("tvp" if (c == Dk - 1 and dh == 1) else None))
                P.wait("C", "tvp")
                if t >= 2:
                    P.wait("C", f"yst{t % 2}")
                P.emit("C", lambda e, t=t, pp=pp: e.activation(
                    vstg[t % 2][:, :], ps_at[pp][:, :], AF.Copy),
                    inc="cvp")
                P.wait("Y", "cvp")
                P.emit("Y", lambda e, t=t: e.dma_start(
                    out=cin_rows[t], in_=vstg[t % 2][:, :]),
                    inc=f"yst{t % 2}", inc_n=16)
                if t % 4 == 3:
                    j = t // 4
                    P.wait("G", "yst0", 16 * (t // 2 + 1))
                    P.wait("G", "yst1", 16 * (t // 2 + 1))
                    P.emit("G", lambda e, j=j: e.collective_compute(
                        "AllGather", ALU.bypass,
                        replica_groups=[list(range(NCORE))],
                        ins=[cc_in.ap()[j * 512:(j + 1) * 512, :].opt()],
                        outs=[ccq_out[j].ap().opt()]), inc="gccq", inc_n=1)
            P.wait("Y", "gcc", 1)
            for r in range(NCORE):
                P.emit("Y", lambda e, r=r: e.dma_start(
                    out=kT_all[0:R, r * TC:(r + 1) * TC],
                    in_=cck_out.ap()[r * R:(r + 1) * R, :]), inc="ykt", inc_n=16)
            pes.close()
            vbuf0 = sbt("vbuf0", [128, 4 * D], BF16)
            vbuf1 = sbt("vbuf1", [128, 4 * D], BF16)
            Pb0 = sbt("Pb0", [128, 512], BF16)
            Pb1 = sbt("Pb1", [128, 512], BF16)
            Pb2 = sbt("Pb2", [128, 512], BF16)
            Pb3 = sbt("Pb3", [128, 512], BF16)
            PTb0 = sbt("PTb0", [128, 512], BF16)
            PTb1 = sbt("PTb1", [128, 512], BF16)
            vbuf = [vbuf0, vbuf1]
            Pb = [Pb0, Pb1, Pb2, Pb3]
            PTb = [PTb0, PTb1]

            P.wait("T", "ykt")
            nblocks = NT // QB

            def scoreA(tt, u, bank):
                P.emit("T", lambda e, tt=tt, u=u, bank=bank: e.matmul(
                    ps_sc[bank][:, :], qT_sb[0:R, tt * 128:(tt + 1) * 128],
                    kT_all[0:R, u * 512:(u + 1) * 512],
                    start=True, stop=True), inc="sca")
                P.wait("V", "sca")
                P.emit("V", lambda e, tt=tt, u=u, bank=bank: e.reduce_max(
                    mbufA[:, tt * NU + u: tt * NU + u + 1], ps_sc[bank][:, :],
                    axis=AX.X), inc="red")
                return P.cnt["red"]

            redm = {}
            for u in range(NU):
                for j in range(QB):
                    if u >= 1:
                        P.wait("T", "red", redm[(j, u - 1)])
                    redm[(j, u)] = scoreA(j, u, j)
            for j in range(QB):
                P.wait("V", "red")
                P.emit("V", lambda e, j=j: e.reduce_max(
                    st(NGMA)[:, j:j + 1], mbufA[:, j * NU:(j + 1) * NU],
                    axis=AX.X, negate=True), inc="vngma")

            vstream = 0
            vfree = []
            for qb in range(nblocks):
                tiles = [qb * QB + j for j in range(QB)]
                nxt = [qb * QB + QB + j for j in range(QB)] if qb + 1 < nblocks else None
                P.wait("T", "red")
                P.wait("T", "ptr")
                P.wait("T", "cex")
                P.wait("T", "vptc")
                P.wait("T", "vevat")
                P.wait("C", "vngma")
                P.wait("C", "ptr")
                scbm = {}
                cexm = {}
                ptrm = {}
                vptcm = {}
                mmam = {}
                redm = {}
                vloadm = {}
                for u in range(NU + 1):
                    if nxt is not None and u >= 1:
                        for j in range(QB):
                            P.wait("T", "cex", cexm[(j, u - 1)])
                            redm[(j, u - 1)] = scoreA(nxt[j], u - 1, j)
                    if u >= 1:
                        up = u - 1
                        for j, t in enumerate(tiles):
                            if up >= 1:
                                P.wait("T", "vptc", vptcm[(j, up - 1)])
                            P.wait("T", "cex", cexm[(j, up)])
                            pslot = (up % 2) * 2 + j
                            for us in range(4):
                                P.emit("T", lambda e, pslot=pslot, us=us, j=j: e.transpose(
                                    ps_pt[j][:, us * 128:(us + 1) * 128],
                                    Pb[pslot][:, us * 128:(us + 1) * 128], idbf_sb[:, :]),
                                    inc=("ptr" if us == 3 else None))
                            ptrm[(j, up)] = P.cnt["ptr"]
                            P.wait("V", "ptr", ptrm[(j, up)])
                            if up >= 1:
                                P.wait("V", "mma", mmam[(j, up - 1)])
                            P.emit("V", lambda e, j=j: e.tensor_copy(
                                PTb[j][:, :], ps_pt[j][:, :]), inc="vptc")
                            vptcm[(j, up)] = P.cnt["vptc"]
                    if u < NU:
                        vp = vstream % 2
                        if vstream >= 2:
                            P.wait("Y", vfree[vstream - 2][0], vfree[vstream - 2][1])
                        P.wait("Y", "gccq", (u % 4) + 1)
                        P.emit("Y", lambda e, u=u, vp=vp: e.dma_start(
                            out=vbuf[vp][:, :].rearrange("p (s d) -> p s d", s=4),
                            in_=ccq_out[u % 4].ap()[(u // 4) * 512:(u // 4 + 1) * 512, :]
                            .rearrange("(s p) d -> p s d", p=128)), inc=f"yv{vp}", inc_n=16)
                        vloadm[u] = (f"yv{vp}", P.cnt[f"yv{vp}"], vp)
                        vstream += 1
                        for j, t in enumerate(tiles):
                            if u >= 1:
                                if nxt is not None:
                                    P.wait("T", "red", redm[(j, u - 1)])
                                else:
                                    P.wait("T", "cex", cexm[(j, u - 1)])
                            P.emit("T", lambda e, t=t, u=u, j=j: e.matmul(
                                ps_sc[j][:, :], qT_sb[0:R, t * 128:(t + 1) * 128],
                                kT_all[0:R, u * 512:(u + 1) * 512],
                                start=True, stop=True), inc="scb")
                            scbm[(j, u)] = P.cnt["scb"]
                            P.wait("C", "scb", scbm[(j, u)])
                            if u >= 2:
                                P.wait("C", "ptr", ptrm[(j, u - 1)])
                            pslot = (u % 2) * 2 + j
                            P.emit("C", lambda e, t=t, u=u, j=j, pslot=pslot: e.activation(
                                Pb[pslot][:, :], ps_sc[j][:, :], AF.Exp,
                                bias=st(NGMA)[:, t:t + 1], scale=1.0,
                                accum_out=lbuf[:, t * NU + u: t * NU + u + 1]),
                                inc="cex")
                            cexm[(j, u)] = P.cnt["cex"]
                    if u >= 1:
                        up = u - 1
                        for j, t in enumerate(tiles):
                            P.wait("T", "vptc", vptcm[(j, up)])
                            P.wait("T", vloadm[up][0], vloadm[up][1])
                            vp = vloadm[up][2]
                            for dh in range(2):
                                for us in range(4):
                                    fin = (us == 3 and dh == 1)
                                    P.emit("T", lambda e, t=t, us=us, dh=dh, j=j, vp=vp,
                                           up=up: e.matmul(
                                        ps_at[t % 2][:, dh * 512:(dh + 1) * 512],
                                        PTb[j][:, us * 128:(us + 1) * 128],
                                        vbuf[vp][:, us * D + dh * 512: us * D + (dh + 1) * 512],
                                        start=(up == 0 and us == 0),
                                        stop=(up == NU - 1 and us == 3)),
                                        inc=("mma" if fin else None))
                            mmam[(j, up)] = P.cnt["mma"]
                        vfree.append(("mma", mmam[(QB - 1, up)]))
                if nxt is not None:
                    for j in range(QB):
                        P.wait("T", "cex", cexm[(j, NU - 1)])
                        redm[(j, NU - 1)] = scoreA(nxt[j], NU - 1, j)
                if nxt is not None:
                    for j in range(QB):
                        P.wait("V", "red", redm[(j, NU - 1)])
                        P.emit("V", lambda e, j=j, nt_=nxt[j]: e.reduce_max(
                            st(NGMA)[:, nt_:nt_ + 1], mbufA[:, nt_ * NU:(nt_ + 1) * NU],
                            axis=AX.X, negate=True), inc="vngma")
                for j, t in enumerate(tiles):
                    P.wait("V", "cex", cexm[(j, NU - 1)])
                    P.emit("V", lambda e, t=t: e.reduce_sum(
                        st(LSUM)[:, t:t + 1], lbuf[:, t * NU:(t + 1) * NU], axis=AX.X),
                        inc="vch")
                    P.wait("V", "vch")
                    P.emit("V", lambda e, t=t: e.reciprocal(
                        st(RECL)[:, t:t + 1], st(LSUM)[:, t:t + 1]), inc="vch")
                    P.wait("V", "vch")
                    P.wait("V", "mma", mmam[(j, NU - 1)])
                    P.emit("V", lambda e, t=t: e.scalar_tensor_tensor(
                        x_sb[:, t * D:(t + 1) * D], ps_at[t % 2][:, :],
                        st(RECL)[:, t:t + 1], x_sb[:, t * D:(t + 1) * D],
                        ALU.mult, ALU.add), inc="vevat")

    P.wait("C", "s_x", 16)
    P.wait("V", "s_x", 16)
    P.wait("T", "s_const")
    emit_b1 = True
    if emit_b1:
        import contextlib as _ctx
        with _ctx.ExitStack() as bes:
            ps_b1a = bes.enter_context(nc.psum_tensor("ps_b1a", [128, 512], F32))
            ps_b1b = bes.enter_context(nc.psum_tensor("ps_b1b", [128, 512], F32))
            b1nat = bes.enter_context(nc.sbuf_tensor("b1nat", [E, H], F32))
            P.emit("Y", lambda e: e.dma_start(out=b1nat[:, :], in_=b1_t),
                   inc="s_b1", inc_n=16)
            P.wait("T", "s_b1", 16)
            ps_b1 = [ps_b1a, ps_b1b]
            P.wait("T", "s_const")
            P.emit("T", lambda e: e.matmul(
                ps_b1a[:, 0:E], onesf_sb[0:1, :], bgr_sb[0:1, 0:E],
                start=True, stop=True), inc="b1tr")
            P.wait("V", "b1tr")
            P.emit("V", lambda e: e.tensor_copy(bgb_sb[:, :], ps_b1a[:, 0:E]),
                   inc="b1cp")
            for hc in range(NHC):
                par = hc % 2
                P.wait("T", "b1cp", 1 + max(0, hc - 1))
                P.emit("T", lambda e, hc=hc, par=par: e.transpose(
                    ps_b1[par][0:128, 0:E], b1nat[0:E, hc * 128:(hc + 1) * 128],
                    idf_sb[0:E, 0:E]), inc="b1tr")
                P.wait("V", "b1tr")
                P.emit("V", lambda e, hc=hc, par=par: e.tensor_copy(
                    b1T[:, hc * E:(hc + 1) * E], ps_b1[par][0:128, 0:E]), inc="b1cp")

    emit_ffn_phase()
    P.gate()
    emit_attn_phase()
    P.gate()
    emit_ffn_phase(store_out=True)
    P.wait("Y", "s_out", 16 * NT)

    with nc.Block() as block:
        @block.tensor
        def _(t):
            P.replay("T", t)

        @block.vector
        def _(v):
            P.replay("V", v)

        @block.scalar
        def _(c):
            P.replay("C", c)

        @block.gpsimd
        def _(g):
            P.replay("G", g)

        @block.sync
        def _(y):
            P.replay("Y", y)

    est.close()
    return nc


def kernel(L0_attn_x, L0_attn_Q, L0_attn_K, L0_attn_VO, L0_ffn_Wg, L0_ffn_bg,
           L0_ffn_W1, L0_ffn_b1, L0_ffn_W2, L0_ffn_b2):
    cfg = CFG
    NCORE, TC, D = cfg["NCORE"], cfg["TC"], cfg["D"]
    x = np.ascontiguousarray(np.asarray(L0_attn_x, dtype=np.float32).reshape(NCORE * TC, D))
    common = {
        "attn_Q": np.ascontiguousarray(np.asarray(L0_attn_Q, dtype=np.float32)),
        "attn_K": np.ascontiguousarray(np.asarray(L0_attn_K, dtype=np.float32)),
        "attn_VO": np.ascontiguousarray(np.asarray(L0_attn_VO, dtype=np.float32)),
        "ffn_Wg": np.ascontiguousarray(np.asarray(L0_ffn_Wg, dtype=np.float32)),
        "ffn_bg": np.ascontiguousarray(np.asarray(L0_ffn_bg, dtype=np.float32)),
        "ffn_W1": np.ascontiguousarray(np.asarray(L0_ffn_W1, dtype=np.float32)),
        "ffn_b1": np.ascontiguousarray(np.asarray(L0_ffn_b1, dtype=np.float32)),
        "ffn_W2": np.ascontiguousarray(np.asarray(L0_ffn_W2, dtype=np.float32)),
        "ffn_b2": np.ascontiguousarray(np.asarray(L0_ffn_b2, dtype=np.float32)),
    }
    nc = build_program(cfg)
    in_maps = []
    for c in range(NCORE):
        m = dict(common)
        m["x_in"] = np.ascontiguousarray(x[c * TC:(c + 1) * TC])
        in_maps.append(m)
    res = run_bass_kernel_spmd(nc, in_maps, core_ids=list(range(NCORE)))
    out = np.concatenate([res.results[c]["x_outp"] for c in range(NCORE)], axis=0)
    return out.reshape(1, NCORE * TC, D)
